# version 02 (final)
# baseline (speedup 1.0000x reference)
import numpy as np
import ml_dtypes
import concourse.bass as bass
import concourse.mybir as mybir
from concourse.bass_utils import run_bass_kernel_spmd

F32 = mybir.dt.float32
BF16 = mybir.dt.bfloat16
AF = mybir.ActivationFunctionType
ALU = mybir.AluOpType
AX = mybir.AxisListType

D = 2048
TC = 2048
TB = 1024
TA = 512
H = 1408
HS = 2816
E = 16
NK = 16
EPS = 1e-5
BF = ml_dtypes.bfloat16
HGS = [(0, 2), (2, 2), (4, 2), (6, 2), (8, 2), (10, 1)]


class Prog:
    ENGS = ("sync", "gpsimd", "vector", "scalar", "tensor")

    def __init__(self):
        self.ops = {e: [] for e in self.ENGS}
        self.cnt = {}
        self.lastwait = {e: {} for e in self.ENGS}

    def c(self, sem):
        return self.cnt.get(sem, 0)

    def emit(self, eng, fn, waits=(), inc=None, incval=1):
        ws = []
        for wv in waits:
            if wv is None:
                continue
            s, v = wv
            if v <= 0:
                continue
            assert v <= self.cnt.get(s, 0), (eng, s, v, self.cnt.get(s, 0))
            if self.lastwait[eng].get(s, -1) >= v:
                continue
            self.lastwait[eng][s] = v
            ws.append((s, v))
        if inc is not None:
            self.cnt[inc] = self.cnt.get(inc, 0) + incval
        self.ops[eng].append((fn, tuple(ws), inc, incval))
        return self.cnt.get(inc, 0) if inc is not None else None

    def barrier(self, engs=("sync", "vector", "scalar", "tensor")):
        tgt = self.c("bar") + len(engs)
        for e in engs:
            self.emit(e, lambda E: E.nop(), inc="bar")
        for e in engs:
            self.ops[e].append((None, (("bar", tgt),), None, 1))
            self.lastwait[e]["bar"] = tgt
        return tgt


def prep_inputs(inputs):
    g = {}
    g["w_in"] = np.ascontiguousarray(inputs["L0_attn_W_in"])
    g["w_out"] = np.ascontiguousarray(inputs["L0_attn_W_out"])
    g["b_in"] = np.ascontiguousarray(
        np.asarray(inputs["L0_attn_b_in"]).reshape(48, 128).T)
    g["b_out"] = np.ascontiguousarray(
        np.asarray(inputs["L0_attn_b_out"]).reshape(16, 128).T)
    g["gamma"] = np.ascontiguousarray(
        np.asarray(inputs["L0_attn_gamma"]).reshape(16, 128).T)
    g["beta"] = np.ascontiguousarray(
        np.asarray(inputs["L0_attn_beta"]).reshape(16, 128).T)
    g["gate_wt"] = np.ascontiguousarray(np.asarray(inputs["L0_ffn_gate_w"]).T)
    g["gate_b8"] = np.ascontiguousarray(
        np.tile(np.asarray(inputs["L0_ffn_gate_b"]), 8)[None, :])
    g["w1t"] = np.ascontiguousarray(
        np.asarray(inputs["L0_ffn_w1"]).transpose(0, 2, 1))
    g["w3t"] = np.ascontiguousarray(
        np.asarray(inputs["L0_ffn_w3"]).transpose(0, 2, 1))
    g["w2t"] = np.ascontiguousarray(
        np.asarray(inputs["L0_ffn_w2"]).transpose(0, 2, 1))
    g["b1"] = np.ascontiguousarray(
        np.asarray(inputs["L0_ffn_b1"]).reshape(E, 11, 128).transpose(2, 0, 1)
        .reshape(128, E * 11))
    g["b3"] = np.ascontiguousarray(
        np.asarray(inputs["L0_ffn_b3"]).reshape(E, 11, 128).transpose(2, 0, 1)
        .reshape(128, E * 11))
    g["b2"] = np.ascontiguousarray(np.asarray(inputs["L0_ffn_b2"]))
    g["ws1t"] = np.ascontiguousarray(np.asarray(inputs["L0_ffn_ws1"]).T)
    g["ws3t"] = np.ascontiguousarray(np.asarray(inputs["L0_ffn_ws3"]).T)
    g["ws2t"] = np.ascontiguousarray(np.asarray(inputs["L0_ffn_ws2"]).T)
    g["bs1"] = np.ascontiguousarray(
        np.asarray(inputs["L0_ffn_bs1"]).reshape(22, 128).T)
    g["bs3"] = np.ascontiguousarray(
        np.asarray(inputs["L0_ffn_bs3"]).reshape(22, 128).T)
    g["bs2"] = np.ascontiguousarray(
        np.asarray(inputs["L0_ffn_bs2"]).reshape(16, 128).T)

    g["ident32"] = np.eye(128, dtype=np.float32)
    psw = np.zeros((128, 128), dtype=np.float32)
    psw[np.arange(128) ^ 1, np.arange(128)] = 1.0
    g["pswap"] = psw.astype(BF)
    sel = np.zeros((E, E * 128), dtype=np.float32)
    for e in range(E):
        sel[e, e * 128:(e + 1) * 128] = 1.0
    g["selmat"] = sel

    rot = 1024
    inv = 1.0 / (10000.0 ** (np.arange(0, rot, 2, dtype=np.float64) / rot))
    pos = np.arange(16384, dtype=np.float64)
    fr = pos[:, None] * inv[None, :]
    cosv = np.cos(fr)
    sinv = np.sin(fr)
    ctab = np.repeat(cosv, 2, axis=1).T
    stab = np.empty((rot, 16384))
    stab[0::2] = -sinv.T
    stab[1::2] = sinv.T

    X = np.asarray(inputs["L0_attn_X"])[0]
    per = []
    for c in range(8):
        sl = slice(c * TC, (c + 1) * TC)
        per.append({
            "x": np.ascontiguousarray(X[sl].T.astype(np.float32)),
            "ctab": np.ascontiguousarray(ctab[:, sl]).astype(BF),
            "stab": np.ascontiguousarray(stab[:, sl]).astype(BF),
        })
    for k in g:
        if g[k].dtype == np.float64:
            g[k] = g[k].astype(np.float32)
        g[k] = np.ascontiguousarray(g[k])
    return g, per


def build_nc(stop=None, n_exp=E, experts=None):
    nc = bass.Bass("TRN2", target_bir_lowering=False,
                   detect_race_conditions=False)
    din = {}

    def I(name, shape, dt=F32):
        din[name] = nc.dram_tensor(name, list(shape), dt, kind="ExternalInput")
        return din[name]

    x_in = I("x", (D, TC))
    w_in = I("w_in", (D, 3 * D))
    w_out = I("w_out", (D, D))
    b_in_t = I("b_in", (128, 48))
    b_out_t = I("b_out", (128, 16))
    gamma_t = I("gamma", (128, 16))
    beta_t = I("beta", (128, 16))
    gate_wt = I("gate_wt", (D, 16))
    gate_b8 = I("gate_b8", (1, 128))
    w1t = I("w1t", (E, D, H))
    w3t = I("w3t", (E, D, H))
    w2t = I("w2t", (E, H, D))
    b1t = I("b1", (128, E * 11))
    b3t = I("b3", (128, E * 11))
    b2t = I("b2", (E, D))
    ws1t = I("ws1t", (D, HS))
    ws3t = I("ws3t", (D, HS))
    ws2t = I("ws2t", (HS, D))
    bs1t = I("bs1", (128, 22))
    bs3t = I("bs3", (128, 22))
    bs2t = I("bs2", (128, 16))
    ident32 = I("ident32", (128, 128))
    pswap_t = I("pswap", (128, 128), BF16)
    selmat_t = I("selmat", (E, E * 128))
    ctab_t = I("ctab", (1024, TC), BF16)
    stab_t = I("stab", (1024, TC), BF16)

    out = nc.dram_tensor("out", [D, TC], F32, kind="ExternalOutput")
    dbg = (nc.dram_tensor("dbg", [D, TC], BF16, kind="ExternalOutput")
           if stop is not None else None)
    s_scr = nc.dram_tensor("s_scr", [D, TC], F32)
    q_scr = nc.dram_tensor("q_scr", [D, TC], BF16)
    cc_in = nc.dram_tensor("cc_in", [128, 16], F32)
    cc_out = nc.dram_tensor("cc_out", [128, 16], F32)

    p = Prog()

    from contextlib import ExitStack
    with ExitStack() as _st:
        sb_x = _st.enter_context(nc.sbuf_tensor("sb_x", [128, NK * TB], F32))
        sb_act = _st.enter_context(nc.sbuf_tensor("sb_act", [128, NK * TB], BF16))
        sb_u = _st.enter_context(nc.sbuf_tensor("sb_u", [128, 8192], F32))
        ringA = _st.enter_context(nc.sbuf_tensor("ringA", [128, 3 * NK * 256], BF16))
        ringB = _st.enter_context(nc.sbuf_tensor("ringB", [128, 2 * 11 * 256], BF16))
        cbc = _st.enter_context(nc.sbuf_tensor("cbc", [128, TB], F32))
        lnscr = _st.enter_context(nc.sbuf_tensor("lnscr", [128, TB], F32))
        rows = _st.enter_context(nc.sbuf_tensor("rows", [1, 512], F32))
        rows2 = _st.enter_context(nc.sbuf_tensor("rows2", [1, 512], F32))
        rt = _st.enter_context(nc.sbuf_tensor("rt", [128, 1024], F32))
        rcols = _st.enter_context(nc.sbuf_tensor("rcols", [128, 24], F32))
        cw_fm = _st.enter_context(nc.sbuf_tensor("cw_fm", [E, 8 * 128], F32))
        cw_fmb = _st.enter_context(nc.sbuf_tensor("cw_fmb", [E, 8 * 128], BF16))
        krg = _st.enter_context(nc.sbuf_tensor("krg", [128, 2 * 512], BF16))
        vrg = _st.enter_context(nc.sbuf_tensor("vrg", [128, 2 * 512], BF16))
        sb_ident = _st.enter_context(nc.sbuf_tensor("sb_ident", [128, 128], F32))
        sb_psw = _st.enter_context(nc.sbuf_tensor("sb_psw", [128, 128], BF16))
        sb_sel = _st.enter_context(nc.sbuf_tensor("sb_sel", [E, E * 128], F32))
        sb_b2 = _st.enter_context(nc.sbuf_tensor("sb_b2", [E, D], BF16))
        sb_b1 = _st.enter_context(nc.sbuf_tensor("sb_b1", [128, E * 11], F32))
        sb_b3 = _st.enter_context(nc.sbuf_tensor("sb_b3", [128, E * 11], F32))
        sb_bs1 = _st.enter_context(nc.sbuf_tensor("sb_bs1", [128, 22], F32))
        sb_bs3 = _st.enter_context(nc.sbuf_tensor("sb_bs3", [128, 22], F32))
        sb_bs2h = _st.enter_context(nc.sbuf_tensor("sb_bs2h", [128, 16], F32))
        sb_bs2raw = _st.enter_context(nc.sbuf_tensor("sb_bs2raw", [128, 16], F32))
        sb_gw = _st.enter_context(nc.sbuf_tensor("sb_gw", [128, NK * 16], BF16))
        sb_gb = _st.enter_context(nc.sbuf_tensor("sb_gb", [128, 128], F32))
        sb_bin = _st.enter_context(nc.sbuf_tensor("sb_bin", [128, 48], F32))
        sb_bout = _st.enter_context(nc.sbuf_tensor("sb_bout", [128, 16], F32))
        sb_gam = _st.enter_context(nc.sbuf_tensor("sb_gam", [128, 16], F32))
        sb_bet = _st.enter_context(nc.sbuf_tensor("sb_bet", [128, 16], F32))
        sb_ones = _st.enter_context(nc.sbuf_tensor("sb_ones", [128, 1], BF16))
        sb_ones32 = _st.enter_context(nc.sbuf_tensor("sb_ones32", [1, 128], F32))
        sb_onesc32 = _st.enter_context(nc.sbuf_tensor("sb_onesc32", [128, 1], F32))
        sb_A = _st.enter_context(nc.sbuf_tensor("sb_A", [128, 16], F32))
        sb_Aacc = _st.enter_context(nc.sbuf_tensor("sb_Aacc", [128, 16], F32))
        sb_cst = _st.enter_context(nc.sbuf_tensor("sb_cst", [128, 4], F32))
        psA0 = _st.enter_context(nc.psum_tensor("psA0", [128, 512], F32))
        psA1 = _st.enter_context(nc.psum_tensor("psA1", [128, 512], F32))
        psA2 = _st.enter_context(nc.psum_tensor("psA2", [128, 512], F32))
        psA3 = _st.enter_context(nc.psum_tensor("psA3", [128, 512], F32))
        psB0 = _st.enter_context(nc.psum_tensor("psB0", [128, 512], F32))
        psB1 = _st.enter_context(nc.psum_tensor("psB1", [128, 512], F32))
        psB2 = _st.enter_context(nc.psum_tensor("psB2", [128, 512], F32))
        psB3 = _st.enter_context(nc.psum_tensor("psB3", [128, 512], F32))
        S_wa = _st.enter_context(nc.semaphore("wa"))
        S_wb = _st.enter_context(nc.semaphore("wb"))
        S_xd = _st.enter_context(nc.semaphore("xd"))
        S_sp = _st.enter_context(nc.semaphore("sp"))
        S_mm = _st.enter_context(nc.semaphore("mm"))
        S_ev = _st.enter_context(nc.semaphore("ev"))
        S_sc = _st.enter_context(nc.semaphore("sc"))
        S_gd = _st.enter_context(nc.semaphore("gd"))
        S_cc = _st.enter_context(nc.semaphore("cc"))
        S_bar = _st.enter_context(nc.semaphore("bar"))
        block = _st.enter_context(nc.Block())
        sems = {"wa": S_wa, "wb": S_wb, "xd": S_xd, "sp": S_sp, "mm": S_mm,
                "ev": S_ev, "sc": S_sc, "gd": S_gd, "cc": S_cc, "bar": S_bar}
        PS = [psA0, psA1, psA2, psA3, psB0, psB1, psB2, psB3]
        tabbuf = ringB[:, 0:4096]
        psfree = [None] * 8
        lnb = lnscr.bitcast(BF16)
        hb = sb_u.bitcast(BF16)

        def xt(k, t0=0, n=TB):
            return sb_x[:, k * TB + t0: k * TB + t0 + n]

        def af(k, t0=0, n=TB):
            return sb_act[:, k * TB + t0: k * TB + t0 + n]

        def an(k):
            return sb_act[:, k * TA: (k + 1) * TA]

        def pa(k):
            return sb_act[:, 8192 + k * TA: 8192 + (k + 1) * TA]

        def hB(j, t0, n):
            return hb[:, j * TB + t0: j * TB + t0 + n]

        def h1g(j, t0, n):
            return hb[:, 11264 + j * TB + t0: 11264 + j * TB + t0 + n]

        def h3g(j, t0, n):
            return hb[:, 13312 + j * TB + t0: 13312 + j * TB + t0 + n]

        def xl(k):
            return sb_u[:, k * TA: (k + 1) * TA]

        ringA_st = {"i": 0, "evict": [0, 0, 0]}
        ringB_st = {"i": 0, "evict": [0, 0]}

        def loadA(src2d, c0, c1):
            st = ringA_st
            slot = st["i"] % 3
            st["i"] += 1
            n = c1 - c0
            ssl = src2d[:, c0:c1].rearrange("(k p) n -> p k n", p=128)
            dst = ringA[:, slot * NK * 256: slot * NK * 256 + NK * n]
            dsl = dst.rearrange("p (k n) -> p k n", n=n)
            p.emit("gpsimd", lambda EE, d=dsl, s=ssl: EE.dma_start(out=d, in_=s),
                   waits=[("mm", st["evict"][slot])], inc="wa", incval=16)
            return slot, dst, p.c("wa"), n

        def doneA(slot):
            ringA_st["evict"][slot] = p.c("mm")

        def loadB(src2d, c0, c1, rows11):
            st = ringB_st
            slot = st["i"] % 2
            st["i"] += 1
            n = c1 - c0
            ssl = src2d[:, c0:c1].rearrange("(k p) n -> p k n", p=128)
            dst = ringB[:, slot * 11 * 256: slot * 11 * 256 + 11 * n]
            dsl = dst.rearrange("p (k n) -> p k n", n=n)
            p.emit("gpsimd", lambda EE, d=dsl, s=ssl: EE.dma_start(out=d, in_=s),
                   waits=[("mm", st["evict"][slot])], inc="wb", incval=16)
            return slot, dst, p.c("wb"), n

        def doneB(slot):
            ringB_st["evict"][slot] = p.c("mm")

        def const_loads():
            pairs = [
                (sb_ident[:, :], ident32[:, :]), (sb_psw[:, :], pswap_t[:, :]),
                (sb_sel[:, :], selmat_t[:, :]),
                (sb_b1[:, :], b1t[:, :]), (sb_b3[:, :], b3t[:, :]),
                (sb_bs1[:, :], bs1t[:, :]), (sb_bs3[:, :], bs3t[:, :]),
                (sb_bs2raw[:, :], bs2t[:, :]), (sb_bin[:, :], b_in_t[:, :]),
                (sb_bout[:, :], b_out_t[:, :]), (sb_gam[:, :], gamma_t[:, :]),
                (sb_bet[:, :], beta_t[:, :]),
                (rows[0:1, 0:128], gate_b8[:, :]),
            ]
            for d, s in pairs:
                p.emit("sync", lambda EE, d=d, s=s: EE.dma_start(out=d, in_=s),
                       inc="xd", incval=16)
            p.emit("gpsimd",
                   lambda EE: EE.dma_start(
                       out=sb_gw.ap().rearrange("p (k n) -> p k n", k=NK),
                       in_=gate_wt.rearrange("(k p) n -> p k n", p=128)),
                   inc="gd", incval=16)
            p.emit("gpsimd",
                   lambda EE: EE.dma_start(out=sb_b2[:, :], in_=b2t[:, :]),
                   inc="gd", incval=16)
            xdw = p.c("xd")
            p.emit("vector", lambda EE: EE.memset(sb_ones[:, :], 1.0), inc="ev")
            p.emit("vector", lambda EE: EE.memset(sb_ones32[:, :], 1.0),
                   inc="ev")
            p.emit("vector", lambda EE: EE.memset(sb_onesc32[:, :], 1.0),
                   inc="ev")
            p.emit("vector", lambda EE: EE.memset(sb_Aacc[:, :], 0.0), inc="ev")
            p.emit("vector", lambda EE: EE.memset(sb_cst[:, 0:1], float(EPS)),
                   inc="ev")
            p.emit("vector", lambda EE: EE.memset(sb_cst[:, 1:2], -0.5),
                   inc="ev")
            p.emit("vector",
                   lambda EE: EE.tensor_scalar_mul(sb_bs2h[:, :],
                                                   sb_bs2raw[:, :], 0.5),
                   waits=[("xd", xdw)], inc="ev")
            evw = p.c("ev")
            p.emit("tensor",
                   lambda EE: EE.matmul(psB0[:, 0:128], sb_ones32[0:1, :],
                                        rows[0:1, 0:128], start=True,
                                        stop=True),
                   waits=[("xd", xdw), ("ev", evw)], inc="mm")
            mmw = p.c("mm")
            p.emit("vector",
                   lambda EE: EE.tensor_copy(sb_gb[:, :], psB0[:, 0:128]),
                   waits=[("mm", mmw)], inc="ev")
            psfree[4] = ("ev", p.c("ev"))

        def ln_stats(src, n, guard=None):
            ntau = n // 512
            base_mm = p.c("mm")
            for tau in range(ntau):
                psS1 = (psA0, psA2)[tau]
                psS2 = (psA1, psA3)[tau]
                w1 = psfree[[0, 2][tau]]
                w2 = psfree[[1, 3][tau]]
                for k in range(NK):
                    stg = (k % 2) * 512
                    c = tau * NK + k
                    pace = base_mm + (2 * (c - 1) if c >= 2 else 0)
                    evb = p.emit("vector",
                                 lambda EE, s=src, k=k, tau=tau, stg=stg:
                                 EE.tensor_copy(lnb[:, stg:stg + 512],
                                                s(k)[:, tau * 512:
                                                     tau * 512 + 512]),
                                 waits=[("mm", pace)], inc="ev")
                    scb = p.emit("scalar",
                                 lambda EE, s=src, k=k, tau=tau, stg=stg:
                                 EE.activation(lnb[:, 1024 + stg: 1536 + stg],
                                               s(k)[:, tau * 512:
                                                    tau * 512 + 512],
                                               AF.Square),
                                 waits=[("mm", pace)] +
                                 ([guard] if guard and k == 0 and tau == 0
                                  else []),
                                 inc="sc")
                    p.emit("tensor",
                           lambda EE, ps=psS1, k=k, stg=stg:
                           EE.matmul(ps[0:1, 0:512], sb_ones[:, 0:1],
                                     lnb[:, stg:stg + 512],
                                     start=(k == 0), stop=(k == NK - 1)),
                           waits=[("ev", evb)] + ([w1] if k == 0 else []),
                           inc="mm")
                    p.emit("tensor",
                           lambda EE, ps=psS2, k=k, stg=stg:
                           EE.matmul(ps[0:1, 0:512], sb_ones[:, 0:1],
                                     lnb[:, 1024 + stg: 1536 + stg],
                                     start=(k == 0), stop=(k == NK - 1)),
                           waits=[("sc", scb)] + ([w2] if k == 0 else []),
                           inc="mm")
            mmrows = p.c("mm")
            for tau in range(ntau):
                psS1 = (psA0, psA2)[tau]
                psS2 = (psA1, psA3)[tau]
                p.emit("vector",
                       lambda EE, ps=psS1:
                       EE.tensor_scalar_mul(rows[0:1, 0:512], ps[0:1, 0:512],
                                            1.0 / D),
                       waits=[("mm", p.c("mm"))], inc="ev")
                p.emit("vector",
                       lambda EE:
                       EE.tensor_mul(rows2[0:1, 0:512], rows[0:1, 0:512],
                                     rows[0:1, 0:512]),
                       inc="ev")
                p.emit("vector",
                       lambda EE, ps=psS2:
                       EE.scalar_tensor_tensor(rows2[0:1, 0:512],
                                               ps[0:1, 0:512], 1.0 / D,
                                               rows2[0:1, 0:512],
                                               op0=ALU.mult, op1=ALU.subtract),
                       inc="ev")
                evv = p.c("ev")
                psfree[[0, 2][tau]] = ("ev", evv)
                psfree[[1, 3][tau]] = ("ev", evv)
                p.emit("scalar",
                       lambda EE: EE.activation(rows2[0:1, 0:512],
                                                rows2[0:1, 0:512],
                                                AF.Ln, bias=sb_cst[0:1, 0:1]),
                       waits=[("ev", evv)], inc="sc")
                p.emit("scalar",
                       lambda EE: EE.activation(rows2[0:1, 0:512],
                                                rows2[0:1, 0:512],
                                                AF.Exp, scale=sb_cst[0:1, 1:2]),
                       inc="sc")
                scr = p.c("sc")
                p.emit("tensor",
                       lambda EE, tau=tau:
                       EE.matmul(PS[4 + tau][:, :], sb_ones32[0:1, :],
                                 rows[0:1, 0:512], start=True, stop=True),
                       waits=[("ev", evv), psfree[4 + tau]], inc="mm")
                p.emit("tensor",
                       lambda EE, tau=tau:
                       EE.matmul(PS[6 + tau][:, :], sb_ones32[0:1, :],
                                 rows2[0:1, 0:512], start=True, stop=True),
                       waits=[("sc", scr), psfree[6 + tau]], inc="mm")
            return p.c("mm")

        def routing(xf_ev, gd_done):
            for tt in range(8):
                for k in range(NK):
                    p.emit("tensor",
                           lambda EE, tt=tt, k=k:
                           EE.matmul(psA0[:, tt * 16: tt * 16 + 16],
                                     sb_act[:, k * TB + tt * 128:
                                            k * TB + tt * 128 + 128],
                                     sb_gw[:, k * 16: k * 16 + 16],
                                     start=(k == 0), stop=(k == NK - 1)),
                           waits=([("ev", xf_ev), ("gd", gd_done), psfree[0]]
                                  if k == 0 else []),
                           inc=("mm" if k == NK - 1 else None))
            mms = p.c("mm")
            if stop == "rt":
                for tt in range(8):
                    p.emit("vector",
                           lambda EE, tt=tt:
                           EE.tensor_copy(rt[:, 640 + tt * 16: 656 + tt * 16],
                                          psA0[:, tt * 16: tt * 16 + 16]),
                           waits=[("mm", mms)], inc="ev")
            RT_probs = rt[:, 0:128]
            RT_sel = rt[:, 128:256]
            RT_work = rt[:, 256:384]
            RT_mask = rt[:, 384:512]
            RT_cw = rt[:, 512:640]
            for tt in range(8):
                p.emit("vector",
                       lambda EE, tt=tt:
                       EE.reduce_max(rcols[:, tt:tt + 1],
                                     psA0[:, tt * 16: tt * 16 + 16], axis=AX.X),
                       waits=[("mm", mms)], inc="ev")
            p.emit("vector",
                   lambda EE: EE.tensor_scalar_mul(rcols[:, 8:16],
                                                   rcols[:, 0:8], -1.0),
                   waits=[("ev", p.c("ev"))], inc="ev")
            evm = p.c("ev")
            for tt in range(8):
                p.emit("scalar",
                       lambda EE, tt=tt:
                       EE.activation(RT_probs[:, tt * 16: tt * 16 + 16],
                                     psA0[:, tt * 16: tt * 16 + 16], AF.Exp,
                                     bias=rcols[:, 8 + tt: 9 + tt]),
                       waits=[("ev", evm)], inc="sc")
            scx = p.c("sc")
            psfree[0] = ("sc", scx)
            if stop == "rt":
                p.emit("vector",
                       lambda EE: EE.tensor_copy(rt[:, 768:896],
                                                 RT_probs[:, :]),
                       waits=[("sc", scx)], inc="ev")
                p.emit("vector",
                       lambda EE: EE.tensor_copy(rt[:, 896:920],
                                                 rcols[:, 0:24]),
                       inc="ev")
            for tt in range(8):
                p.emit("vector",
                       lambda EE, tt=tt:
                       EE.reduce_sum(rcols[:, tt:tt + 1],
                                     RT_probs[:, tt * 16: tt * 16 + 16],
                                     axis=AX.X),
                       waits=[("sc", scx)], inc="ev")
            p.emit("vector",
                   lambda EE: EE.reciprocal(rcols[:, 8:16], rcols[:, 0:8]),
                   waits=[("ev", p.c("ev"))], inc="ev")
            p.emit("vector", lambda EE: EE.engine_nop(),
                   waits=[("ev", p.c("ev"))], inc="ev")
            for tt in range(8):
                p.emit("vector",
                       lambda EE, tt=tt:
                       EE.tensor_scalar(RT_probs[:, tt * 16: tt * 16 + 16],
                                        RT_probs[:, tt * 16: tt * 16 + 16],
                                        rcols[:, 8 + tt: 9 + tt], None,
                                        op0=ALU.mult),
                       inc="ev")
            p.emit("vector",
                   lambda EE: EE.tensor_add(RT_work[:, :], RT_probs[:, :],
                                            sb_gb[:, :]),
                   inc="ev")
            p.emit("vector", lambda EE: EE.memset(RT_sel[:, :], 0.0), inc="ev")
            for it in range(6):
                for tt in range(8):
                    p.emit("vector",
                           lambda EE, tt=tt:
                           EE.reduce_max(rcols[:, tt:tt + 1],
                                         RT_work[:, tt * 16: tt * 16 + 16],
                                         axis=AX.X),
                           inc="ev")
                for tt in range(8):
                    p.emit("vector",
                           lambda EE, tt=tt:
                           EE.tensor_scalar(RT_mask[:, tt * 16: tt * 16 + 16],
                                            RT_work[:, tt * 16: tt * 16 + 16],
                                            rcols[:, tt:tt + 1], None,
                                            op0=ALU.is_equal),
                           waits=[("ev", p.c("ev"))] if tt == 0 else [],
                           inc="ev")
                p.emit("vector",
                       lambda EE: EE.tensor_add(RT_sel[:, :], RT_sel[:, :],
                                                RT_mask[:, :]),
                       inc="ev")
                p.emit("vector",
                       lambda EE: EE.scalar_tensor_tensor(
                           RT_work[:, :], RT_mask[:, :], -1e30, RT_work[:, :],
                           op0=ALU.mult, op1=ALU.add),
                       inc="ev")
            p.emit("vector",
                   lambda EE: EE.scalar_tensor_tensor(
                       RT_cw[:, :], RT_probs[:, :], 0.5, RT_sel[:, :],
                       op0=ALU.mult, op1=ALU.mult),
                   inc="ev")
            evcw = p.c("ev")
            mmt = []
            for tt in range(8):
                psi = 4 + tt // 4
                p.emit("tensor",
                       lambda EE, tt=tt, psi=psi:
                       EE.transpose(
                           PS[psi][0:16, (tt % 4) * 128:(tt % 4) * 128 + 128],
                           RT_cw[:, tt * 16: tt * 16 + 16], sb_ident[:, :]),
                       waits=[("ev", evcw), psfree[psi]] if tt % 4 == 0 else
                       [("ev", evcw)],
                       inc="mm")
            mmt = p.c("mm")
            p.emit("vector",
                   lambda EE: EE.tensor_copy(cw_fm[0:16, 0:512],
                                             psB0[0:16, :]),
                   waits=[("mm", mmt)], inc="ev")
            psfree[4] = ("ev", p.c("ev"))
            p.emit("vector",
                   lambda EE: EE.tensor_copy(cw_fm[0:16, 512:1024],
                                             psB1[0:16, :]),
                   inc="ev")
            psfree[5] = ("ev", p.c("ev"))
            p.emit("vector",
                   lambda EE: EE.tensor_copy(cw_fmb[0:16, :], cw_fm[0:16, :]),
                   inc="ev")
            return p.c("ev")

        def ffn(gd_done, act_guard=None, wsrc=None,
                x_guard=None):
            (sw1, sw3, sw2, sws1, sws3, sws2) = wsrc
            mmbc = ln_stats(lambda k: xt(k), TB, guard=x_guard)
            for k in range(NK):
                for tau in range(2):
                    p.emit("vector",
                           lambda EE, k=k, tau=tau:
                           EE.tensor_sub(lnscr[:, 0:512],
                                         xt(k, tau * 512, 512),
                                         PS[4 + tau][:, :]),
                           waits=[("mm", mmbc)] + (
                               [act_guard] if act_guard and k == 0 and tau == 0
                               else []),
                           inc="ev")
                    p.emit("vector",
                           lambda EE, k=k, tau=tau:
                           EE.tensor_mul(af(k, tau * 512, 512), lnscr[:, 0:512],
                                         PS[6 + tau][:, :]),
                           inc="ev")
            xf_ev = p.c("ev")
            for i in (4, 5, 6, 7):
                psfree[i] = ("ev", xf_ev)
            if stop == "ln":
                p.emit("sync",
                       lambda EE: EE.dma_start(
                           out=dbg[:, 0:TB].rearrange("(k p) n -> p k n",
                                                      p=128),
                           in_=sb_act.ap().rearrange("p (k n) -> p k n",
                                                     k=NK)),
                       waits=[("ev", xf_ev)], inc="sp", incval=16)
                return None
            cw_ev = routing(xf_ev, gd_done)
            if stop == "rt":
                p.emit("sync",
                       lambda EE: EE.dma_start(
                           out=dbg[0:128, 0:2048],
                           in_=rt.ap().bitcast(BF16)),
                       waits=[("ev", cw_ev)], inc="sp", incval=16)
                p.emit("sync",
                       lambda EE: EE.dma_start(
                           out=dbg[128:144, 0:2048],
                           in_=cw_fm.ap().bitcast(BF16)),
                       inc="sp", incval=16)
                return None

            prev_out_mm = [0]
            hmul_ev = {}

            def expert(eid):
                is_sh = eid >= E
                half = eid - E
                prev_out = prev_out_mm[0]
                cbc_ev = 0
                if not is_sh:
                    mmc = []
                    for tau in range(2):
                        psi = 6 + tau
                        p.emit("tensor",
                               lambda EE, tau=tau, psi=psi, eid=eid:
                               EE.matmul(PS[psi][:, :],
                                         sb_sel[0:16,
                                                eid * 128: eid * 128 + 128],
                                         cw_fm[0:16,
                                               tau * 512: tau * 512 + 512],
                                         start=True, stop=True),
                               waits=[("ev", cw_ev), psfree[psi]], inc="mm")
                        mmc.append(p.c("mm"))
                    for tau in range(2):
                        p.emit("vector",
                               lambda EE, tau=tau:
                               EE.tensor_copy(cbc[:, tau * 512: tau * 512 + 512],
                                              PS[6 + tau][:, :]),
                               waits=[("mm", mmc[tau])], inc="ev")
                        psfree[6 + tau] = ("ev", p.c("ev"))
                    cbc_ev = p.c("ev")

                silu_sc = {}
                h_ev_t = {}
                for gi, (h0, nh) in enumerate(HGS):
                    if is_sh:
                        sA = loadA(sws1, half * H + h0 * 128,
                                   half * H + (h0 + nh) * 128)
                        sB = loadA(sws3, half * H + h0 * 128,
                                   half * H + (h0 + nh) * 128)
                    else:
                        sA = loadA(sw1[eid], h0 * 128, (h0 + nh) * 128)
                        sB = loadA(sw3[eid], h0 * 128, (h0 + nh) * 128)
                    slA, bufA, rdyA, _ = sA
                    slB, bufB, rdyB, _ = sB
                    pair = (gi % 2) * 2
                    for j in range(nh):
                        ht = h0 + j
                        for tau in range(2):
                            psi = pair + tau
                            for k in range(NK):
                                p.emit("tensor",
                                       lambda EE, psi=psi, k=k, tau=tau, j=j,
                                       bufA=bufA, nh=nh:
                                       EE.matmul(
                                           PS[psi][:, :],
                                           bufA[:, (k * nh + j) * 128:
                                                (k * nh + j) * 128 + 128],
                                           sb_act[:, k * TB + tau * 512:
                                                  k * TB + tau * 512 + 512],
                                           start=(k == 0), stop=(k == NK - 1)),
                                       waits=([("wa", rdyA), ("ev", xf_ev),
                                               psfree[psi]] if k == 0 else []),
                                       inc=("mm" if k == NK - 1 else None))
                            mmh = p.c("mm")
                            bsl = (sb_bs1[:, half * 11 + ht: half * 11 + ht + 1]
                                   if is_sh else
                                   sb_b1[:, eid * 11 + ht: eid * 11 + ht + 1])
                            p.emit("scalar",
                                   lambda EE, psi=psi, j=j, tau=tau, b=bsl:
                                   EE.activation(h1g(j, tau * 512, 512),
                                                 PS[psi][:, :], AF.Silu,
                                                 bias=b),
                                   waits=[("mm", mmh),
                                          ("ev", hmul_ev.get((j, tau), 0))],
                                   inc="sc")
                            silu_sc[(j, tau)] = p.c("sc")
                            psfree[psi] = ("sc", p.c("sc"))
                    doneA(slA)
                    for j in range(nh):
                        ht = h0 + j
                        for tau in range(2):
                            psi = pair + tau
                            for k in range(NK):
                                p.emit("tensor",
                                       lambda EE, psi=psi, k=k, tau=tau, j=j,
                                       bufB=bufB, nh=nh:
                                       EE.matmul(
                                           PS[psi][:, :],
                                           bufB[:, (k * nh + j) * 128:
                                                (k * nh + j) * 128 + 128],
                                           sb_act[:, k * TB + tau * 512:
                                                  k * TB + tau * 512 + 512],
                                           start=(k == 0), stop=(k == NK - 1)),
                                       waits=([("wa", rdyB), psfree[psi]]
                                              if k == 0 else []),
                                       inc=("mm" if k == NK - 1 else None))
                            mmh3 = p.c("mm")
                            if is_sh:
                                bsl = sb_bs3[:, half * 11 + ht:
                                             half * 11 + ht + 1]
                                p.emit("vector",
                                       lambda EE, psi=psi, j=j, tau=tau, b=bsl:
                                       EE.tensor_scalar(
                                           h3g(j, tau * 512, 512),
                                           PS[psi][:, :], b, 0.5,
                                           op0=ALU.add, op1=ALU.mult),
                                       waits=[("mm", mmh3)], inc="ev")
                            else:
                                bsl = sb_b3[:, eid * 11 + ht:
                                            eid * 11 + ht + 1]
                                p.emit("vector",
                                       lambda EE, psi=psi, j=j, tau=tau, b=bsl:
                                       EE.scalar_tensor_tensor(
                                           h3g(j, tau * 512, 512),
                                           PS[psi][:, :], b,
                                           cbc[:, tau * 512: tau * 512 + 512],
                                           op0=ALU.add, op1=ALU.mult),
                                       waits=[("mm", mmh3), ("ev", cbc_ev)],
                                       inc="ev")
                            psfree[psi] = ("ev", p.c("ev"))
                            p.emit("vector",
                                   lambda EE, ht=ht, j=j, tau=tau:
                                   EE.tensor_mul(hB(ht, tau * 512, 512),
                                                 h1g(j, tau * 512, 512),
                                                 h3g(j, tau * 512, 512)),
                                   waits=[("sc", silu_sc[(j, tau)]),
                                          ("mm", prev_out)],
                                   inc="ev")
                            hmul_ev[(j, tau)] = p.c("ev")
                            h_ev_t[ht] = p.c("ev")
                    doneA(slB)
                h_ev = p.c("ev")

                rot = 0
                for dg in range(8):
                    if is_sh:
                        sC = loadB(sws2[half * H:(half + 1) * H, :],
                                   dg * 256, dg * 256 + 256, 11)
                    else:
                        sC = loadB(sw2[eid], dg * 256, dg * 256 + 256, 11)
                    slC, bufC, rdyC, _ = sC
                    for dt in range(2):
                        for tau in range(2):
                            psi = 4 + (rot % 4)
                            rot += 1
                            first = True
                            if is_sh and half == 0:
                                p.emit("tensor",
                                       lambda EE, psi=psi, dg=dg, dt=dt,
                                       tau=tau:
                                       EE.matmul(
                                           PS[psi][:, :],
                                           sb_b2[0:16, (dg * 2 + dt) * 128:
                                                 (dg * 2 + dt) * 128 + 128],
                                           cw_fmb[0:16, tau * 512:
                                                  tau * 512 + 512],
                                           start=True, stop=False),
                                       waits=[("wb", rdyC), ("ev", h_ev),
                                              psfree[psi], ("ev", cw_ev)],
                                       inc=None)
                                first = False
                            fine = (dg == 0 and dt == 0 and tau == 0)
                            for kk in range(11):
                                if fine:
                                    ww = ([("wb", rdyC), psfree[psi]]
                                          if kk == 0 else []) +                                         [("ev", h_ev_t[kk])]
                                else:
                                    ww = ([("wb", rdyC), ("ev", h_ev),
                                           psfree[psi]] if kk == 0 else [])
                                p.emit("tensor",
                                       lambda EE, psi=psi, kk=kk, dt=dt,
                                       tau=tau, bufC=bufC:
                                       EE.matmul(
                                           PS[psi][:, :],
                                           bufC[:, kk * 256 + dt * 128:
                                                kk * 256 + dt * 128 + 128],
                                           hB(kk, tau * 512, 512),
                                           start=(first and kk == 0),
                                           stop=(kk == 10)),
                                       waits=ww,
                                       inc=("mm" if kk == 10 else None))
                            mmo = p.c("mm")
                            kd = dg * 2 + dt
                            if is_sh and half == 1:
                                p.emit("vector",
                                       lambda EE, psi=psi, kd=kd, tau=tau:
                                       EE.scalar_tensor_tensor(
                                           xt(kd, tau * 512, 512),
                                           PS[psi][:, :],
                                           sb_bs2h[:, kd: kd + 1],
                                           xt(kd, tau * 512, 512),
                                           op0=ALU.add, op1=ALU.add),
                                       waits=[("mm", mmo)], inc="ev")
                            else:
                                p.emit("vector",
                                       lambda EE, psi=psi, kd=kd, tau=tau:
                                       EE.tensor_add(xt(kd, tau * 512, 512),
                                                     xt(kd, tau * 512, 512),
                                                     PS[psi][:, :]),
                                       waits=[("mm", mmo)], inc="ev")
                            psfree[psi] = ("ev", p.c("ev"))
                    doneB(slC)
                prev_out_mm[0] = p.c("mm")

            elist = (experts if experts is not None
                     else list(range(n_exp)) + [E, E + 1])
            for eid in elist:
                expert(eid)
            return p.c("ev")

        def attn_front(sb, hf):
            t0g = sb * TB + hf * TA
            t0 = hf * TA

            tabfree = [0, 0]
            tab_xd_g = {}

            def load_tab(g):
                slot = g % 2
                fa = tabfree[slot]
                p.emit("sync",
                       lambda EE, g=g, slot=slot: EE.dma_start(
                           out=ringB[:, slot * 2048: slot * 2048 + 1024]
                           .rearrange("p (k n) -> p k n", k=2),
                           in_=ctab_t[g * 256:(g + 1) * 256,
                                      t0g: t0g + TA].rearrange(
                               "(k p) n -> p k n", p=128)),
                       waits=[("ev", fa)], inc="xd", incval=16)
                p.emit("sync",
                       lambda EE, g=g, slot=slot: EE.dma_start(
                           out=ringB[:, slot * 2048 + 1024:
                                     slot * 2048 + 2048]
                           .rearrange("p (k n) -> p k n", k=2),
                           in_=stab_t[g * 256:(g + 1) * 256,
                                      t0g: t0g + TA].rearrange(
                               "(k p) n -> p k n", p=128)),
                       waits=[("ev", fa)], inc="xd", incval=16)
                tab_xd_g[g] = p.c("xd")

            mmbc = ln_stats(lambda k: xt(k, t0, TA), TA)
            for k in range(NK):
                p.emit("vector",
                       lambda EE, k=k:
                       EE.tensor_sub(lnscr[:, 0:512], xt(k, t0, TA),
                                     psB0[:, :]),
                       waits=[("mm", mmbc)], inc="ev")
                p.emit("vector",
                       lambda EE, k=k:
                       EE.tensor_mul(xl(k), lnscr[:, 0:512], psB2[:, :]),
                       inc="ev")
            xl_ev = p.c("ev")
            psfree[4] = psfree[6] = ("ev", xl_ev)
            sadd_start = p.c("ev")
            for k2 in range(0, NK, 2):
                sp_prev = p.c("sp")
                for j in range(2):
                    k = k2 + j
                    p.emit("vector",
                           lambda EE, k=k, j=j:
                           EE.tensor_add(cbc[:, j * 512: j * 512 + 512],
                                         xt(k, t0, TA), xl(k)),
                           waits=[("sp", sp_prev)] if j == 0 else [],
                           inc="ev")
                evs = p.c("ev")
                dsl = s_scr[k2 * 128:(k2 + 2) * 128, t0g: t0g + TA].rearrange(
                    "(k p) n -> p k n", p=128)
                p.emit("scalar",
                       lambda EE, dsl=dsl:
                       EE.dma_start(out=dsl, in_=cbc.ap().rearrange(
                           "p (k n) -> p k n", k=2)),
                       waits=[("ev", evs)], inc="sp", incval=16)
            sp_s = p.c("sp")

            mmbc2 = ln_stats(lambda k: xl(k), TA)
            for k in range(NK):
                p.emit("vector",
                       lambda EE, k=k:
                       EE.tensor_sub(lnscr[:, 0:512], xl(k), psB0[:, :]),
                       waits=[("mm", mmbc2)] +
                       ([("sp", sp_s)] if k == 0 else []),
                       inc="ev")
                p.emit("vector",
                       lambda EE, k=k:
                       EE.tensor_mul(lnscr[:, 512:1024], lnscr[:, 0:512],
                                     psB2[:, :]),
                       inc="ev")
                p.emit("vector",
                       lambda EE, k=k:
                       EE.tensor_scalar(an(k), lnscr[:, 512:1024],
                                        sb_gam[:, k:k + 1], sb_bet[:, k:k + 1],
                                        op0=ALU.mult, op1=ALU.add),
                       inc="ev")
            xn_ev = p.c("ev")
            psfree[4] = psfree[6] = ("ev", xn_ev)

            qspill_guard = [0]

            def mgroup(cols0, kind, g):
                sA = loadA(w_in, cols0, cols0 + 256)
                slA, bufA, rdyA, _ = sA
                evac_done = []
                for j in range(2):
                    psi = (2 * g + j) % 4
                    for k in range(NK):
                        p.emit("tensor",
                               lambda EE, psi=psi, k=k, j=j, bufA=bufA:
                               EE.matmul(PS[psi][:, :],
                                         bufA[:, (k * 2 + j) * 128:
                                              (k * 2 + j) * 128 + 128],
                                         an(k), start=(k == 0),
                                         stop=(k == NK - 1)),
                               waits=([("wa", rdyA), ("ev", xn_ev),
                                       psfree[psi]] if k == 0 else []),
                               inc=("mm" if k == NK - 1 else None))
                    mmq = p.c("mm")
                    mt = (cols0 - {"q": 0, "k": 2048, "v": 4096}[kind]) // 128 \
                        + j
                    binc = {"q": 0, "k": 16, "v": 32}[kind] + mt
                    dst = {"k": krg, "v": vrg, "q": krg}[kind]
                    if kind == "q" or mt >= 8:
                        p.emit("vector",
                               lambda EE, psi=psi, j=j, dst=dst, binc=binc:
                               EE.tensor_scalar(dst[:, j * 512: j * 512 + 512],
                                                PS[psi][:, :],
                                                sb_bin[:, binc:binc + 1], None,
                                                op0=ALU.add),
                               waits=[("mm", mmq),
                                      ("sp", qspill_guard[0])],
                               inc="ev")
                        psfree[psi] = ("ev", p.c("ev"))
                    else:
                        kg = mt // 2
                        slot = kg % 2
                        p.emit("vector",
                               lambda EE, psi=psi, binc=binc:
                               EE.tensor_scalar(lnb[:, 0:512], PS[psi][:, :],
                                                sb_bin[:, binc:binc + 1], None,
                                                op0=ALU.add),
                               waits=[("mm", mmq)],
                               inc="ev")
                        evkb = p.c("ev")
                        psfree[psi] = ("ev", evkb)
                        p.emit("tensor",
                               lambda EE:
                               EE.matmul(psB1[:, :], sb_psw[:, :],
                                         lnb[:, 0:512], start=True, stop=True),
                               waits=[("ev", evkb), psfree[5]], inc="mm")
                        mmsw = p.c("mm")
                        p.emit("vector",
                               lambda EE, j=j, slot=slot:
                               EE.tensor_mul(
                                   cbc[:, 0:512], lnb[:, 0:512],
                                   ringB[:, slot * 2048 + j * 512:
                                         slot * 2048 + j * 512 + 512]),
                               waits=[("xd", tab_xd_g[kg])], inc="ev")
                        p.emit("vector",
                               lambda EE, j=j, slot=slot:
                               EE.tensor_mul(
                                   cbc[:, 512:1024], psB1[:, :],
                                   ringB[:, slot * 2048 + 1024 + j * 512:
                                         slot * 2048 + 1024 + j * 512 + 512]),
                               waits=[("mm", mmsw)], inc="ev")
                        psfree[5] = ("ev", p.c("ev"))
                        p.emit("vector",
                               lambda EE, j=j, dst=dst:
                               EE.tensor_add(dst[:, j * 512: j * 512 + 512],
                                             cbc[:, 0:512], cbc[:, 512:1024]),
                               inc="ev")
                        if j == 1:
                            tabfree[slot] = p.c("ev")
                    evac_done.append(p.c("ev"))
                doneA(slA)
                return evac_done[-1]

            def sq_acc(dst_off, srcg, g, prev_ev):
                for j in range(2):
                    p.emit("vector",
                           lambda EE, j=j, srcg=srcg:
                           EE.tensor_mul(lnscr[:, 0:512],
                                         srcg[:, j * 512: j * 512 + 512],
                                         srcg[:, j * 512: j * 512 + 512]),
                           waits=[("ev", prev_ev)], inc="ev")
                    if g == 0 and j == 0:
                        p.emit("vector",
                               lambda EE:
                               EE.tensor_copy(rt[:, dst_off:dst_off + 512],
                                              lnscr[:, 0:512]),
                               inc="ev")
                    else:
                        p.emit("vector",
                               lambda EE:
                               EE.tensor_add(rt[:, dst_off:dst_off + 512],
                                             rt[:, dst_off:dst_off + 512],
                                             lnscr[:, 0:512]),
                               inc="ev")
                return p.c("ev")

            for g in range(8):
                if g < 4:
                    load_tab(g)
                evk = mgroup(2048 + g * 256, "k", 2 * g)
                evkq = sq_acc(0, krg, g, evk)
                evv = mgroup(4096 + g * 256, "v", 2 * g + 1)
                evvq = sq_acc(512, vrg, g, evv)
                for j in range(2):
                    p.emit("vector",
                           lambda EE, g=g, j=j:
                           EE.tensor_mul(pa(2 * g + j),
                                         krg[:, j * 512: j * 512 + 512],
                                         vrg[:, j * 512: j * 512 + 512]),
                           waits=[("ev", max(evkq, evvq))], inc="ev")
            evn = p.c("ev")
            p.emit("tensor",
                   lambda EE: EE.matmul(psA0[0:1, 0:512], sb_onesc32[:, 0:1],
                                        rt[:, 0:512], start=True, stop=True),
                   waits=[("ev", evn), psfree[0]], inc="mm")
            p.emit("tensor",
                   lambda EE: EE.matmul(psA1[0:1, 0:512], sb_onesc32[:, 0:1],
                                        rt[:, 512:1024], start=True, stop=True),
                   waits=[("ev", evn), psfree[1]], inc="mm")
            mmn = p.c("mm")
            p.emit("vector",
                   lambda EE: EE.tensor_copy(rows2[0:1, 0:512],
                                             psA0[0:1, 0:512]),
                   waits=[("mm", mmn)], inc="ev")
            psfree[0] = ("ev", p.c("ev"))
            p.emit("vector",
                   lambda EE: EE.tensor_mul(rows[0:1, 0:512],
                                            rows2[0:1, 0:512],
                                            psA1[0:1, 0:512]),
                   inc="ev")
            psfree[1] = ("ev", p.c("ev"))
            evw = p.c("ev")
            p.emit("scalar",
                   lambda EE: EE.activation(rows[0:1, 0:512], rows[0:1, 0:512],
                                            AF.Ln),
                   waits=[("ev", evw)], inc="sc")
            p.emit("scalar",
                   lambda EE: EE.activation(rows[0:1, 0:512], rows[0:1, 0:512],
                                            AF.Exp, scale=sb_cst[0:1, 1:2]),
                   inc="sc")
            scw = p.c("sc")
            p.emit("tensor",
                   lambda EE: EE.matmul(psB0[:, :], sb_ones32[0:1, :],
                                        rows[0:1, 0:512], start=True,
                                        stop=True),
                   waits=[("sc", scw), psfree[4]], inc="mm")
            mmw = p.c("mm")
            for k in range(NK):
                p.emit("vector",
                       lambda EE, k=k:
                       EE.tensor_mul(lnscr[:, 0:512], pa(k), psB0[:, :]),
                       waits=[("mm", mmw)], inc="ev")
                p.emit("vector",
                       lambda EE, k=k:
                       EE.reduce_sum(lnscr[:, 512:513], lnscr[:, 0:512],
                                     axis=AX.X),
                       inc="ev")
                p.emit("vector",
                       lambda EE, k=k:
                       EE.tensor_add(sb_Aacc[:, k:k + 1], sb_Aacc[:, k:k + 1],
                                     lnscr[:, 512:513]),
                       waits=[("ev", p.c("ev"))], inc="ev")
            psfree[4] = ("ev", p.c("ev"))

            a_part_ev = p.c("ev")
            for g in range(8):
                qspill_guard[0] = p.c("sp")
                evq = mgroup(g * 256, "q", g)
                dsl = q_scr[g * 256:(g + 1) * 256, t0g: t0g + TA].rearrange(
                    "(k p) n -> p k n", p=128)
                p.emit("scalar",
                       lambda EE, dsl=dsl:
                       EE.dma_start(out=dsl, in_=krg.ap().rearrange(
                           "p (k n) -> p k n", k=2)),
                       waits=[("ev", evq)], inc="sp", incval=16)
            return p.c("sp"), a_part_ev, p.c("mm"), p.c("ev")

        def do_collective(ev0=None):
            if ev0 is None:
                ev0 = p.c("ev")
            p.emit("gpsimd",
                   lambda EE: EE.dma_start(out=cc_in[:, :], in_=sb_Aacc[:, :]),
                   waits=[("ev", ev0)], inc="cc", incval=16)
            p.emit("gpsimd",
                   lambda EE: EE.collective_compute(
                       "AllReduce", ALU.add,
                       replica_groups=[list(range(8))],
                       ins=[cc_in.ap().opt()],
                       outs=[cc_out.ap().opt()]),
                   waits=[("cc", p.c("cc"))], inc="cc", incval=1)
            p.emit("gpsimd",
                   lambda EE: EE.dma_start(out=sb_A[:, :], in_=cc_out[:, :]),
                   waits=[("cc", p.c("cc"))], inc="cc", incval=16)
            ccd = p.c("cc")
            p.emit("vector",
                   lambda EE: EE.tensor_scalar(sb_A[:, :], sb_A[:, :], 0.5, 0.5,
                                               op0=ALU.mult, op1=ALU.add),
                   waits=[("cc", ccd)], inc="ev")
            return p.c("ev")

        def emit_f2_loads(sb, wv=(), wq=()):
            t0g = sb * TB
            spall = p.c("sp")
            p.emit("sync",
                   lambda EE: EE.dma_start(
                       out=sb_x.ap().rearrange("p (k n) -> p k n", k=NK),
                       in_=s_scr[:, t0g: t0g + TB].rearrange(
                           "(k p) n -> p k n", p=128)),
                   waits=[("sp", spall)] + list(wv), inc="xd", incval=16)
            s_xd = p.c("xd")
            p.emit("sync",
                   lambda EE: EE.dma_start(
                       out=sb_act.ap().rearrange("p (k n) -> p k n", k=NK),
                       in_=q_scr[:, t0g: t0g + TB].rearrange(
                           "(k p) n -> p k n", p=128)),
                   waits=list(wq), inc="xd", incval=16)
            return s_xd, p.c("xd")

        def f2(sb, a_ev, gd_done, s_xd=None, q_xd=None):
            t0g = sb * TB
            if s_xd is None:
                s_xd, q_xd = emit_f2_loads(sb)
            for k in range(NK):
                p.emit("vector",
                       lambda EE, k=k:
                       EE.tensor_scalar(af(k), af(k), sb_A[:, k:k + 1], None,
                                        op0=ALU.mult),
                       waits=[("xd", q_xd), ("ev", a_ev)], inc="ev")
            aq_ev = p.c("ev")
            rotc = 0
            for dg in range(8):
                sA = loadA(w_out, dg * 256, dg * 256 + 256)
                slA, bufA, rdyA, _ = sA
                for dt in range(2):
                    for tau in range(2):
                        psi = 4 + (rotc % 4)
                        rotc += 1
                        for k in range(NK):
                            p.emit("tensor",
                                   lambda EE, psi=psi, k=k, dt=dt, tau=tau,
                                   bufA=bufA:
                                   EE.matmul(PS[psi][:, :],
                                             bufA[:, (k * 2 + dt) * 128:
                                                  (k * 2 + dt) * 128 + 128],
                                             af(k, tau * 512, 512),
                                             start=(k == 0),
                                             stop=(k == NK - 1)),
                                   waits=([("wa", rdyA), ("ev", aq_ev),
                                           psfree[psi]] if k == 0 else []),
                                   inc=("mm" if k == NK - 1 else None))
                        mmo = p.c("mm")
                        kd = dg * 2 + dt
                        p.emit("vector",
                               lambda EE, psi=psi, kd=kd, tau=tau:
                               EE.scalar_tensor_tensor(
                                   xt(kd, tau * 512, 512), PS[psi][:, :],
                                   sb_bout[:, kd:kd + 1],
                                   xt(kd, tau * 512, 512),
                                   op0=ALU.add, op1=ALU.add),
                               waits=[("mm", mmo), ("xd", s_xd)], inc="ev")
                        psfree[psi] = ("ev", p.c("ev"))
                doneA(slA)
            wout_mm = p.c("mm")
            wout_ev = p.c("ev")
            ffn(gd_done, act_guard=("mm", wout_mm),
                wsrc=(w1t, w3t, w2t, ws1t, ws3t, ws2t),
                x_guard=("ev", wout_ev))
            evf = p.c("ev")
            p.emit("sync",
                   lambda EE: EE.dma_start(
                       out=out[:, t0g: t0g + TB].rearrange(
                           "(k p) n -> p k n", p=128),
                       in_=sb_x.ap().rearrange("p (k n) -> p k n", k=NK)),
                   waits=[("ev", evf)], inc="sp", incval=16)
            return p.c("sp")

        xw0 = load_x_early = None
        p.emit("sync",
               lambda EE: EE.dma_start(
                   out=sb_x.ap().rearrange("p (k n) -> p k n", k=NK),
                   in_=x_in[:, 0:TB].rearrange("(k p) n -> p k n", p=128)),
               inc="xd", incval=16)
        xw0 = p.c("xd")
        const_loads()
        gd_done = p.c("gd")
        p.barrier()

        def load_x(sb):
            p.emit("sync",
                   lambda EE: EE.dma_start(
                       out=sb_x.ap().rearrange("p (k n) -> p k n", k=NK),
                       in_=x_in[:, sb * TB:(sb + 1) * TB].rearrange(
                           "(k p) n -> p k n", p=128)),
                   inc="xd", incval=16)
            return p.c("xd")

        def dump_x(sbi):
            evf = p.c("ev")
            scf = p.c("sc")
            p.emit("sync",
                   lambda EE: EE.dma_start(
                       out=out[:, sbi * TB:(sbi + 1) * TB].rearrange(
                           "(k p) n -> p k n", p=128),
                       in_=sb_x.ap().rearrange("p (k n) -> p k n", k=NK)),
                   waits=[("ev", evf), ("sc", scf)], inc="sp", incval=16)

        def wait_all_dma():
            p.emit("sync", lambda EE: EE.nop(),
                   waits=[("sp", p.c("sp")), ("xd", p.c("xd")),
                          ("wa", p.c("wa")), ("wb", p.c("wb")),
                          ("cc", p.c("cc"))])

        for e in ("tensor", "vector", "scalar"):
            p.ops[e].append((None, (("xd", xw0),), None, 1))
            p.lastwait[e]["xd"] = xw0

        ffn(gd_done, wsrc=(w1t, w3t, w2t, ws1t, ws3t, ws2t))
        if stop in ("ln", "rt"):
            wait_all_dma()
            finalize(p, block, sems)
            return nc
        if stop == "f1a":
            dump_x(0)
            if dbg is not None:
                evf = p.c("ev")
                p.emit("sync",
                       lambda EE: EE.dma_start(
                           out=dbg[0:768, 0:2048].rearrange(
                               "(j p) n -> p j n", p=128),
                           in_=hb[:, 0:12288].rearrange(
                               "p (j n) -> p j n", n=2048)),
                       waits=[("ev", evf)], inc="sp", incval=16)
            wait_all_dma()
            finalize(p, block, sems)
            return nc
        p.barrier()
        attn_front(0, 0)
        p.barrier()
        attn_front(0, 1)
        p.barrier()
        if stop == "ata":
            spw = p.c("sp")
            p.emit("sync",
                   lambda EE: EE.dma_start(out=out[:, :], in_=s_scr[:, :]),
                   waits=[("sp", spw)], inc="sp", incval=16)
            p.emit("sync",
                   lambda EE: EE.dma_start(out=dbg[:, 0:1024],
                                           in_=q_scr[:, 0:1024]),
                   inc="sp", incval=16)
            evq = p.c("ev")
            p.emit("sync",
                   lambda EE: EE.dma_start(out=dbg[0:128, 1024:1056],
                                           in_=sb_Aacc.ap().bitcast(BF16)),
                   waits=[("ev", evq)], inc="sp", incval=16)
            wait_all_dma()
            finalize(p, block, sems)
            return nc

        xw = load_x(1)
        for e in ("tensor", "vector", "scalar"):
            p.ops[e].append((None, (("xd", xw),), None, 1))
            p.lastwait[e]["xd"] = xw
        ffn(gd_done, wsrc=(w1t, w3t, w2t, ws1t, ws3t, ws2t))
        p.barrier()
        attn_front(1, 0)
        p.barrier()
        _, a_part_ev, qkv_mm, sadds_ev = attn_front(1, 1)
        a_ev = do_collective(a_part_ev)
        sxd, qxd = emit_f2_loads(0, wv=[("ev", sadds_ev)],
                                 wq=[("mm", qkv_mm)])
        p.barrier()
        if stop == "atb":
            dump_x(1)
            wait_all_dma()
            finalize(p, block, sems)
            return nc

        f2(0, a_ev, gd_done, s_xd=sxd, q_xd=qxd)
        p.barrier()
        if stop == "f2a":
            dump_x(0)
            wait_all_dma()
            finalize(p, block, sems)
            return nc
        f2(1, a_ev, gd_done)
        wait_all_dma()
        finalize(p, block, sems)
    return nc


def finalize(p, block, sems):
    for eng in Prog.ENGS:
        ops = p.ops[eng]

        def emit_stream(E, ops=ops):
            for fn, ws, inc, incval in ops:
                for (s, v) in ws:
                    E.wait_ge(sems[s], v)
                if fn is None:
                    continue
                inst = fn(E)
                if inc is not None and inst is not None:
                    inst.then_inc(sems[inc], incval)
        getattr(block, eng)(emit_stream)


def kernel(**inputs):
    g, per = prep_inputs(inputs)
    nc = build_nc()
    in_maps = []
    for c in range(8):
        m = dict(per[c])
        m.update(g)
        in_maps.append(m)
    res = run_bass_kernel_spmd(nc, in_maps, core_ids=list(range(8)))
    outs = [res.results[c]["out"] for c in range(8)]
    full = np.concatenate([o.T for o in outs], axis=0)
    return np.ascontiguousarray(full.reshape(1, 16384, D).astype(np.float32))
